# version 01
# baseline (speedup 1.0000x reference)
import numpy as np
import concourse.bass as bass
import concourse.mybir as mybir
from concourse.bass_utils import run_bass_kernel_spmd
from contextlib import ExitStack

F32 = mybir.dt.float32
BF16 = mybir.dt.bfloat16
AF = mybir.ActivationFunctionType
OP = mybir.AluOpType
AX = mybir.AxisListType

B, T, C = 8, 2048, 1024
CC = 8
L = 128
NTC = T // L
NE = 15
I = 512
ICC = 4
EPS = 1e-5

PEC, DVC, SCC = 10, 11, 4
DV0, SC0, PE0 = 2, 2, 0
PE1 = PE0 + NTC * PEC
DV1 = DV0 + NTC * DVC
SC1 = SC0 + NTC * SCC
PE2, DV2, SC2 = PE1 + 8, DV1 + 12, SC1 + 4
PE3, DV3, SC3 = PE2 + 36, DV2 + 20, SC2 + 1
PESH = PE3 + 40
DVSH = DV3 + 4 + 48
SCSH = SC3 + 16


def fl(ap):
    return ap.rearrange("p a b -> p (a b)")


def pem(c, k): return PE0 + PEC * c + k
def dvm(c, k): return DV0 + DVC * c + k
def scm(c, k): return SC0 + SCC * c + k

def pe2m(g, k): return PE1 + 2 * g + k
def dv2m(g, k): return DV1 + 3 * g + k
def sc2m(g): return SC1 + g + 1

def pe3_lg(tc): return PE2 + tc + 1
PE3_FWDT = PE2 + 4 + 16
PE3_BACKT = PE2 + 4 + 32
DV3_LGEV = DV2 + 1
DV3_PRESOFT = DV2 + 4
DV3_SMAX = DV2 + 19
DV3_CFMEV = DV2 + 20
SC3_SEXP = SC2 + 1

def pe_conv(tc): return PE3 + 10 * tc + 1
def pe_upsh(tc): return PE3 + 10 * tc + 2
def pe_dnsh(tc, oc): return PE3 + 10 * tc + 3 + oc
def pe_gate(e, ocI): return PESH + 16 * e + 2 * ocI + 1
def pe_up(e, ocI): return PESH + 16 * e + 2 * ocI + 2
def pe_down(e, oc): return PESH + 16 * e + 9 + oc
def dv_id(tc): return DV3 + tc + 1
def dv_hsh(tc, ocI): return DV3 + 4 + 12 * tc + ocI + 1
def dv_dash(tc, oc): return DV3 + 4 + 12 * tc + 5 + oc
def dv_h(e, ocI, tc): return DVSH + 24 * e + 4 * ocI + tc + 1
def dv_dadd(e, oc): return DVSH + 24 * e + 17 + oc
def sc_sish(tc, ocI): return SC3 + 4 * tc + ocI + 1
def sc_silu(e, ocI, tc): return SCSH + 16 * e + 4 * ocI + tc + 1

def sg_h_milestone(idx):
    if idx < 16:
        return dv_hsh(idx // 4, idx % 4)
    j = idx - 16
    return dv_h(j // 16, (j % 16) // 4, j % 4)


def build(stop_after="ffn", taps=(), parts=("id", "sh", "ex")):
    nc = bass.Bass("TRN2", target_bir_lowering=False)

    x_d = nc.dram_tensor("x", [C, T], F32, kind="ExternalInput")
    wk_d = nc.dram_tensor("wk", [C, C], F32, kind="ExternalInput")
    wv_d = nc.dram_tensor("wv", [C, C], F32, kind="ExternalInput")
    wr_d = nc.dram_tensor("wr", [C, C], F32, kind="ExternalInput")
    wo_d = nc.dram_tensor("wo", [C, C], F32, kind="ExternalInput")
    par_d = nc.dram_tensor("par", [128, 40], F32, kind="ExternalInput")
    id_d = nc.dram_tensor("idm", [128, 128], F32, kind="ExternalInput")
    gw_d = nc.dram_tensor("gw", [NE, C, I], F32, kind="ExternalInput")
    uw_d = nc.dram_tensor("uw", [NE, C, I], F32, kind="ExternalInput")
    dw_d = nc.dram_tensor("dw", [NE, I, C], F32, kind="ExternalInput")
    cw_d = nc.dram_tensor("cw", [3, C, I], F32, kind="ExternalInput")
    su_d = nc.dram_tensor("su", [C, I], F32, kind="ExternalInput")
    sd_d = nc.dram_tensor("sd", [I, C], F32, kind="ExternalInput")
    rw_d = nc.dram_tensor("rw", [C, 16], F32, kind="ExternalInput")
    rb_d = nc.dram_tensor("rb", [16, 1], F32, kind="ExternalInput")
    comb_d = nc.dram_tensor("comb_dram", [16, T], F32, kind="Internal")
    out_d = nc.dram_tensor("out", [C, T], F32, kind="ExternalOutput")
    tap_d = {}
    for tname in taps:
        shape = {"acc": [C, T], "ln2x": [C, T + 2], "k": [NTC, C, L], "v": [NTC, C, L],
                 "sr": [NTC, C, L], "y": [NTC, C, L], "P": [NTC, C, L],
                 "comb": [16, T], "lg": [16, T], "cb15": [128, T],
                 "ptm": [128, 256], "ctm": [128, 256],
                 "psv": [128, 256], "ptm2": [128, 256]}[tname]
        dt = BF16 if tname in ("ln2x", "cb15") else F32
        tap_d[tname] = nc.dram_tensor("tap_" + tname, shape, dt, kind="ExternalOutput")

    xv = x_d.rearrange("(cc p) t -> p cc t", p=128)
    outv = out_d.rearrange("(cc p) t -> p cc t", p=128)

    with ExitStack() as st:
        en = st.enter_context
        acc = en(nc.sbuf_tensor("acc", [128, CC, T], F32))
        par = en(nc.sbuf_tensor("par_sb", [128, 40], F32))
        ew = en(nc.sbuf_tensor("ew", [128, CC], F32))
        ones_b = en(nc.sbuf_tensor("ones_b", [128, 128], BF16))
        id_sb = en(nc.sbuf_tensor("id_sb", [128, 128], F32))

        s_pe = en(nc.semaphore("s_pe"))
        s_dv = en(nc.semaphore("s_dv"))
        s_sc = en(nc.semaphore("s_sc"))
        s_dx = en(nc.semaphore("s_dx"))
        s_dw = en(nc.semaphore("s_dw"))
        s_dpar = en(nc.semaphore("s_dpar"))
        s_dwa = en(nc.semaphore("s_dwa"))
        s_dwf = en(nc.semaphore("s_dwf"))
        s_dsl = [en(nc.semaphore(f"s_dsl{i}")) for i in range(4)]
        s_cb = [en(nc.semaphore(f"s_cb{i}")) for i in range(2)]
        s_dx2 = [en(nc.semaphore(f"s_dx2{i}")) for i in range(2)]
        s_dc = en(nc.semaphore("s_dc"))
        s_do = en(nc.semaphore("s_do"))

        U = lambda cc: par[:, cc:cc + 1]
        TMK = lambda cc: par[:, 16 + cc:17 + cc]
        TMV = lambda cc: par[:, 24 + cc:25 + cc]
        TMR = lambda cc: par[:, 32 + cc:33 + cc]

        with ExitStack() as s1:
            e1 = s1.enter_context
            wk = e1(nc.sbuf_tensor("wk_sb", [128, CC, C], BF16))
            wv = e1(nc.sbuf_tensor("wv_sb", [128, CC, C], BF16))
            wr = e1(nc.sbuf_tensor("wr_sb", [128, CC, C], BF16))
            wo = e1(nc.sbuf_tensor("wo_sb", [128, CC, C], BF16))
            ewb = e1(nc.sbuf_tensor("ewb", [128, CC, L], F32))
            x_ch = [e1(nc.sbuf_tensor(f"x_ch{i}", [128, CC, L], F32)) for i in range(2)]
            xb = e1(nc.sbuf_tensor("xb", [128, CC, L], BF16))
            xsq = e1(nc.sbuf_tensor("xsq", [128, CC, L], BF16))
            ln1x = e1(nc.sbuf_tensor("ln1x", [128, CC, L + 1], BF16))
            mxk = e1(nc.sbuf_tensor("mxk", [128, CC, L], BF16))
            mxv = e1(nc.sbuf_tensor("mxv", [128, CC, L], BF16))
            mxr = e1(nc.sbuf_tensor("mxr", [128, CC, L], BF16))
            k_sb = e1(nc.sbuf_tensor("k_sb", [128, CC, L], F32))
            v_sb = e1(nc.sbuf_tensor("v_sb", [128, CC, L], F32))
            e_t = e1(nc.sbuf_tensor("e_t", [128, CC, L], F32))
            a_t = e1(nc.sbuf_tensor("a_t", [128, CC, L], F32))
            ev_t = e1(nc.sbuf_tensor("ev_t", [128, CC, L], F32))
            av_t = e1(nc.sbuf_tensor("av_t", [128, CC, L], F32))
            srl = e1(nc.sbuf_tensor("srl", [128, CC, L], F32))
            t1 = e1(nc.sbuf_tensor("t1", [128, CC, L], F32))
            den = e1(nc.sbuf_tensor("den", [128, CC, L], F32))
            m1_sb = e1(nc.sbuf_tensor("m1_sb", [128, L], F32))
            w1 = e1(nc.sbuf_tensor("w1", [128, CC, L], F32))
            P_ch = e1(nc.sbuf_tensor("P_ch", [128, CC, L + 1], F32))
            Q_ch = e1(nc.sbuf_tensor("Q_ch", [128, CC, L + 1], F32))
            sry = e1(nc.sbuf_tensor("sry", [128, CC, L], BF16))
            pb = [e1(nc.psum_tensor(f"pb{i}", [128, 512], F32)) for i in range(8)]
            blk = e1(nc.Block())

            pS = pb[0][:, 0:128]
            pSQ = pb[0][:, 128:256]
            pM = pb[0][:, 256:384]
            pV = pb[0][:, 384:512]

            @blk.gpsimd
            def _(g):
                g.dma_start(par[:], par_d[:]).then_inc(s_dpar, 16)
                g.dma_start(id_sb[:], id_d[:]).then_inc(s_dpar, 16)
                g.dma_start(wk[:], wk_d.rearrange("(cc p) d -> p cc d", p=128)).then_inc(s_dwa, 16)
                g.dma_start(wv[:], wv_d.rearrange("(cc p) d -> p cc d", p=128)).then_inc(s_dwa, 16)
                g.dma_start(wr[:], wr_d.rearrange("(cc p) d -> p cc d", p=128)).then_inc(s_dwa, 16)
                g.dma_start(wo[:], wo_d.rearrange("(cc p) d -> p cc d", p=128)).then_inc(s_dwa, 16)

            @blk.vector
            def _(v):
                v.wait_ge(s_sc, 1)
                v.tensor_scalar_mul(ew[:], ew[:], -1.0).then_inc(s_dv, 1)
                v.wait_ge(s_sc, 2)
                v.tensor_copy(ewb[:], ew[:].unsqueeze(2).broadcast_to([128, CC, L]))
                v.memset(ln1x[:, :, 0:1], 0.0)
                v.memset(P_ch[:, :, 0:1], 0.0)
                v.memset(Q_ch[:, :, 0:1], 0.0)
                v.memset(ones_b[:], 1.0).then_inc(s_dv, 1)
                for c in range(NTC):
                    xc = x_ch[c % 2]
                    cs = slice(c * L, (c + 1) * L)
                    if c >= 1:
                        v.wait_ge(s_pe, pem(c - 1, 2))
                    v.wait_ge(s_dx2[c % 2], 16 * (c // 2 + 1))
                    v.tensor_copy(xb[:], xc[:]).then_inc(s_dv, 1)
                    v.tensor_tensor(xsq[:], xb[:], xb[:], OP.mult).then_inc(s_dv, 1)
                    v.wait_ge(s_pe, pem(c, 2))
                    v.tensor_scalar_mul(m1_sb[:], pS, 1.0 / C)
                    v.tensor_tensor(w1[:, 0, :], m1_sb[:], m1_sb[:], OP.mult)
                    v.scalar_tensor_tensor(pV, pSQ, 1.0 / C, w1[:, 0, :], OP.mult, OP.subtract)
                    v.tensor_scalar_add(pV, pV, EPS).then_inc(s_dv, 1)
                    v.wait_ge(s_sc, scm(c, 1))
                    if c >= 1:
                        v.tensor_copy(ln1x[:, :, 0:1], ln1x[:, :, L:L + 1])
                    mb = m1_sb[:].unsqueeze(1).broadcast_to([128, CC, L])
                    rs = pV.unsqueeze(1).broadcast_to([128, CC, L])
                    v.tensor_tensor(w1[:], xc[:], mb, OP.subtract)
                    v.tensor_tensor(ln1x[:, :, 1:], w1[:], rs, OP.mult).then_inc(s_dv, 1)
                    v.tensor_tensor(xsq[:], ln1x[:, :, 1:], ln1x[:, :, 0:L], OP.subtract)
                    for mx, tm in ((mxk, TMK), (mxv, TMV), (mxr, TMR)):
                        for cc in range(CC):
                            inst = v.scalar_tensor_tensor(mx[:, cc, :], xsq[:, cc, :], tm(cc),
                                                          ln1x[:, cc, 0:L], OP.mult, OP.add)
                    inst.then_inc(s_dv, 1)
                    if c >= 1:
                        v.wait_ge(s_sc, scm(c - 1, 2))
                    v.wait_ge(s_pe, pem(c, 4))
                    v.tensor_copy(fl(k_sb[:, 0:4, :]), pb[1][:])
                    v.tensor_copy(fl(k_sb[:, 4:8, :]), pb[2][:])
                    v.wait_ge(s_pe, pem(c, 6))
                    v.tensor_copy(fl(v_sb[:, 0:4, :]), pb[3][:])
                    v.tensor_copy(fl(v_sb[:, 4:8, :]), pb[4][:]).then_inc(s_dv, 1)
                    v.wait_ge(s_sc, scm(c, 2))
                    v.tensor_tensor(ev_t[:], e_t[:], v_sb[:], OP.mult)
                    v.tensor_tensor(av_t[:], a_t[:], v_sb[:], OP.mult).then_inc(s_dv, 1)
                    if c >= 1:
                        v.tensor_copy(P_ch[:, :, 0:1], P_ch[:, :, L:L + 1])
                        v.tensor_copy(Q_ch[:, :, 0:1], Q_ch[:, :, L:L + 1])
                    for cc in range(CC):
                        v.tensor_tensor_scan(P_ch[:, cc, 1:], ewb[:, cc, :], ev_t[:, cc, :],
                                             P_ch[:, cc, 0:1], OP.mult, OP.add)
                    for cc in range(CC):
                        inst = v.tensor_tensor_scan(Q_ch[:, cc, 1:], ewb[:, cc, :], e_t[:, cc, :],
                                                    Q_ch[:, cc, 0:1], OP.mult, OP.add)
                    inst.then_inc(s_dv, 1)
                    v.tensor_tensor(den[:], Q_ch[:, :, 0:L], a_t[:], OP.add).then_inc(s_dv, 1)
                    v.tensor_tensor(w1[:], P_ch[:, :, 0:L], av_t[:], OP.add)
                    v.wait_ge(s_sc, scm(c, 4))
                    v.tensor_tensor(w1[:], w1[:], e_t[:], OP.mult)
                    v.tensor_tensor(sry[:], srl[:], w1[:], OP.mult).then_inc(s_dv, 1)
                    v.wait_ge(s_pe, pem(c, 10))
                    v.tensor_tensor(acc[:, 0:4, cs], xc[:, 0:4, :],
                                    pb[7][:].rearrange("p (a b) -> p a b", b=L), OP.add)
                    v.tensor_tensor(acc[:, 4:8, cs], xc[:, 4:8, :],
                                    pb[1][:].rearrange("p (a b) -> p a b", b=L), OP.add).then_inc(s_dv, 1)

            @blk.scalar
            def _(s):
                s.wait_ge(s_dpar, 32)
                s.activation(ew[:], par[:, 8:16], AF.Exp).then_inc(s_sc, 1)
                s.wait_ge(s_dv, 1)
                s.activation(ew[:], ew[:], AF.Exp).then_inc(s_sc, 1)
                for c in range(NTC):
                    s.wait_ge(s_dv, dvm(c, 3))
                    s.activation(pV, pV, AF.Ln)
                    s.activation(pV, pV, AF.Exp, scale=-0.5).then_inc(s_sc, 1)
                    s.wait_ge(s_dv, dvm(c, 6))
                    s.activation(fl(e_t[:]), fl(k_sb[:]), AF.Exp)
                    for cc in range(CC):
                        inst = s.activation(a_t[:, cc, :], k_sb[:, cc, :], AF.Exp, bias=U(cc))
                    inst.then_inc(s_sc, 1)
                    s.wait_ge(s_pe, pem(c, 8))
                    s.activation(fl(t1[:, 0:4, :]), pb[5][:], AF.Exp, scale=-1.0)
                    s.activation(fl(t1[:, 4:8, :]), pb[6][:], AF.Exp, scale=-1.0)
                    s.activation(fl(t1[:]), fl(t1[:]), AF.Ln, bias=1.0)
                    s.activation(fl(srl[:]), fl(t1[:]), AF.Exp, scale=-1.0).then_inc(s_sc, 1)
                    s.wait_ge(s_dv, dvm(c, 9))
                    s.activation(fl(den[:]), fl(den[:]), AF.Ln)
                    s.activation(fl(e_t[:]), fl(den[:]), AF.Exp, scale=-1.0).then_inc(s_sc, 1)

            @blk.tensor
            def _(t):
                t.wait_ge(s_dwa, 16 * 4)
                for c in range(NTC):
                    t.wait_ge(s_dv, dvm(c, 2))
                    for cc in range(CC):
                        inst = t.matmul(pS, ones_b[:], xb[:, cc, :], start=(cc == 0), stop=(cc == CC - 1))
                    inst.then_inc(s_pe, 1)
                    for cc in range(CC):
                        inst = t.matmul(pSQ, ones_b[:], xsq[:, cc, :], start=(cc == 0), stop=(cc == CC - 1))
                    inst.then_inc(s_pe, 1)
                    t.wait_ge(s_dv, dvm(c, 5))
                    if c >= 1:
                        t.wait_ge(s_dv, dvm(c - 1, 11))
                        t.wait_ge(s_sc, scm(c - 1, 3))
                    for (w_sb, mx, b0) in ((wk, mxk, 1), (wv, mxv, 3), (wr, mxr, 5)):
                        for half in range(2):
                            for oc in range(4):
                                od = half * 4 + oc
                                for kc in range(CC):
                                    inst = t.matmul(pb[b0 + half][:, oc * 128:(oc + 1) * 128],
                                                    w_sb[:, kc, od * 128:(od + 1) * 128],
                                                    mx[:, kc, :],
                                                    start=(kc == 0), stop=(kc == CC - 1))
                            inst.then_inc(s_pe, 1)
                    t.wait_ge(s_dv, dvm(c, 10))
                    for half, bank in ((0, 7), (1, 1)):
                        for oc in range(4):
                            od = half * 4 + oc
                            for kc in range(CC):
                                inst = t.matmul(pb[bank][:, oc * 128:(oc + 1) * 128],
                                                wo[:, kc, od * 128:(od + 1) * 128],
                                                sry[:, kc, :],
                                                start=(kc == 0), stop=(kc == CC - 1))
                        inst.then_inc(s_pe, 1)

            @blk.sync
            def _(sy):
                for c in range(NTC):
                    if c >= 2:
                        sy.wait_ge(s_dv, dvm(c - 2, 11))
                    sy.dma_start(x_ch[c % 2][:], xv[:, :, c * L:(c + 1) * L]).then_inc(s_dx2[c % 2], 16)
                    if taps:
                        if "k" in tap_d:
                            sy.wait_ge(s_dv, dvm(c, 6))
                            sy.dma_start(tap_d["k"].rearrange("n (cc p) l -> n p cc l", p=128)[c],
                                         k_sb[:]).then_inc(s_do, 16)
                        if "v" in tap_d:
                            sy.wait_ge(s_dv, dvm(c, 6))
                            sy.dma_start(tap_d["v"].rearrange("n (cc p) l -> n p cc l", p=128)[c],
                                         v_sb[:]).then_inc(s_do, 16)
                        if "sr" in tap_d:
                            sy.wait_ge(s_sc, scm(c, 3))
                            sy.dma_start(tap_d["sr"].rearrange("n (cc p) l -> n p cc l", p=128)[c],
                                         srl[:]).then_inc(s_do, 16)
                        if "P" in tap_d:
                            sy.wait_ge(s_dv, dvm(c, 8))
                            sy.dma_start(tap_d["P"].rearrange("n (cc p) l -> n p cc l", p=128)[c],
                                         P_ch[:, :, 1:]).then_inc(s_do, 16)
                        if "y" in tap_d:
                            sy.wait_ge(s_dv, dvm(c, 10))
                            sy.dma_start(tap_d["y"].rearrange("n (cc p) l -> n p cc l", p=128)[c],
                                         w1[:]).then_inc(s_do, 16)

        n_chunk_taps = NTC * len({"k", "v", "sr", "P", "y"} & set(taps))
        if stop_after == "attn":
            with nc.Block() as blk9:
                @blk9.sync
                def _(sy):
                    sy.dma_start(outv, acc[:]).then_inc(s_do, 16)
                    if "acc" in tap_d:
                        sy.dma_start(tap_d["acc"].rearrange("(cc p) t -> p cc t", p=128),
                                     acc[:]).then_inc(s_do, 16)
                    sy.wait_ge(s_do, 16 * (1 + (1 if "acc" in tap_d else 0) + n_chunk_taps))
            return nc

        with ExitStack() as s2:
            e2 = s2.enter_context
            ln2x = e2(nc.sbuf_tensor("ln2x", [128, CC, T + 2], BF16))
            cwS = e2(nc.sbuf_tensor("cw_sb", [128, 24, I], BF16))
            suS = e2(nc.sbuf_tensor("su_sb", [128, CC, I], BF16))
            sdS = e2(nc.sbuf_tensor("sd_sb", [128, ICC, C], BF16))
            rwS = e2(nc.sbuf_tensor("rw_sb", [128, CC, 16], BF16))
            rbS = e2(nc.sbuf_tensor("rb_sb", [16, 1], F32))
            wsl = [e2(nc.sbuf_tensor(f"wsl{i}", [128, CC, I], BF16)) for i in range(4)]

            def slot_ap(m):
                s = wsl[m % 4]
                if m % 3 == 2:
                    return s[:].rearrange("p a b -> p (a b)").rearrange("p (a b) -> p a b", b=C)
                return s[:]

            with ExitStack() as sb2:
                eb2 = sb2.enter_context
                accb = eb2(nc.sbuf_tensor("accb", [128, CC, 512], BF16))
                xsqb = eb2(nc.sbuf_tensor("xsqb", [128, CC, 512], BF16))
                m_sb = eb2(nc.sbuf_tensor("m_sb", [128, 512], F32))
                vr_sb = eb2(nc.sbuf_tensor("vr_sb", [128, 512], F32))
                sgl = eb2(nc.sbuf_tensor("sgl", [128, 512], F32))
                pc = [eb2(nc.psum_tensor(f"pc{i}", [128, 512], F32)) for i in range(2)]
                blk2 = eb2(nc.Block())

                @blk2.gpsimd
                def _(g):
                    g.dma_start(cwS[:], cw_d.rearrange("k (cc p) o -> p (k cc) o", p=128)).then_inc(s_dwf, 16)
                    g.dma_start(suS[:], su_d.rearrange("(cc p) o -> p cc o", p=128)).then_inc(s_dwf, 16)
                    g.dma_start(sdS[:], sd_d.rearrange("(cc p) o -> p cc o", p=128)).then_inc(s_dwf, 16)
                    g.dma_start(rwS[:], rw_d.rearrange("(cc p) o -> p cc o", p=128)).then_inc(s_dwf, 16)
                    g.dma_start(rbS[:], rb_d[:]).then_inc(s_dwf, 16)
                    for m in range(4):
                        e, j = m // 3, m % 3
                        src = (gw_d, uw_d, dw_d)[j]
                        view = src[e].rearrange("(cc p) o -> p cc o", p=128)
                        g.dma_start(slot_ap(m), view).then_inc(s_dsl[m % 4], 16)

                @blk2.vector
                def _(v):
                    for g_ in range(4):
                        gs = slice(g_ * 512, (g_ + 1) * 512)
                        v.tensor_copy(accb[:], acc[:, :, gs]).then_inc(s_dv, 1)
                        v.tensor_tensor(xsqb[:], accb[:], accb[:], OP.mult)
                        v.wait_ge(s_pe, pe2m(g_, 2))
                        v.tensor_scalar_mul(m_sb[:], pc[0][:], 1.0 / C)
                        v.tensor_tensor(vr_sb[:], m_sb[:], m_sb[:], OP.mult)
                        v.scalar_tensor_tensor(vr_sb[:], pc[1][:], 1.0 / C, vr_sb[:],
                                               OP.mult, OP.subtract)
                        v.tensor_scalar_add(vr_sb[:], vr_sb[:], EPS).then_inc(s_dv, 1)
                        v.wait_ge(s_sc, sc2m(g_))
                        for cc in range(CC):
                            v.tensor_tensor(sgl[:], acc[:, cc, gs], m_sb[:], OP.subtract)
                            inst = v.tensor_tensor(ln2x[:, cc, 2 + g_ * 512: 2 + (g_ + 1) * 512],
                                                   sgl[:], vr_sb[:], OP.mult)
                        inst.then_inc(s_dv, 1)
                    v.memset(ln2x[:, :, 0:2], 0.0)

                @blk2.scalar
                def _(s):
                    for g_ in range(4):
                        s.wait_ge(s_dv, dv2m(g_, 2))
                        s.activation(vr_sb[:], vr_sb[:], AF.Ln)
                        s.activation(vr_sb[:], vr_sb[:], AF.Exp, scale=-0.5).then_inc(s_sc, 1)

                @blk2.tensor
                def _(t):
                    for g_ in range(4):
                        t.wait_ge(s_dv, dv2m(g_, 1))
                        if g_ >= 1:
                            t.wait_ge(s_dv, dv2m(g_ - 1, 3))
                        for cc in range(CC):
                            inst = t.matmul(pc[0][:], ones_b[:], accb[:, cc, :],
                                            start=(cc == 0), stop=(cc == CC - 1))
                        inst.then_inc(s_pe, 1)
                        for cc in range(CC):
                            inst = t.matmul(pc[1][:], ones_b[:], xsqb[:, cc, :],
                                            start=(cc == 0), stop=(cc == CC - 1))
                        inst.then_inc(s_pe, 1)

            if stop_after == "ln2":
                with nc.Block() as blk9:
                    @blk9.sync
                    def _(sy):
                        n = 1 + n_chunk_taps
                        sy.dma_start(outv, acc[:]).then_inc(s_do, 16)
                        if "ln2x" in tap_d:
                            sy.dma_start(tap_d["ln2x"].rearrange("(cc p) t -> p cc t", p=128),
                                         ln2x[:]).then_inc(s_do, 16)
                            n += 1
                        sy.wait_ge(s_do, 16 * n)
                return nc

            with ExitStack() as sb3:
                eb3 = sb3.enter_context
                lg = eb3(nc.sbuf_tensor("lg", [16, T], F32))
                ptm = eb3(nc.sbuf_tensor("ptm", [128, 16, 16], F32))
                ptm2 = eb3(nc.sbuf_tensor("ptm2", [128, 16, 16], F32))
                psv = eb3(nc.sbuf_tensor("psv", [128, 16, 16], F32))
                flsh = eb3(nc.sbuf_tensor("flsh", [128, 16], F32))
                ctm = eb3(nc.sbuf_tensor("ctm", [128, 16, 16], F32))
                cfm = eb3(nc.sbuf_tensor("cfm", [16, T], F32))
                aux = eb3(nc.sbuf_tensor("aux", [128, 16, 4], F32))
                pc3 = [eb3(nc.psum_tensor(f"pc3{i}", [128, 512], F32)) for i in range(8)]
                blk3 = eb3(nc.Block())

                @blk3.tensor
                def _(t):
                    t.wait_ge(s_dwf, 16 * 5)
                    for tc in range(4):
                        for kc in range(CC):
                            inst = t.matmul(pc3[4 + tc][0:16, :], rwS[:, kc, :],
                                            ln2x[:, kc, 2 + tc * 512: 2 + (tc + 1) * 512],
                                            start=(kc == 0), stop=(kc == CC - 1))
                        inst.then_inc(s_pe, 1)
                    t.wait_ge(s_dv, DV3_LGEV)
                    for i in range(16):
                        t.matmul(pc3[0][:, i * 16:(i + 1) * 16],
                                 lg[0:16, i * 128:(i + 1) * 128],
                                 id_sb[0:16, 0:16], start=True, stop=True).then_inc(s_pe, 1)
                    t.wait_ge(s_dv, DV3_SMAX)
                    for i in range(16):
                        t.matmul(pc3[4 + i // 4][0:16, (i % 4) * 128:(i % 4 + 1) * 128],
                                 fl(ctm)[:, i * 16:(i + 1) * 16],
                                 id_sb[:, 0:128], start=True, stop=True).then_inc(s_pe, 1)

                @blk3.vector
                def _(v):
                    v.wait_ge(s_pe, pe3_lg(3))
                    for tc in range(4):
                        inst = v.tensor_scalar_add(lg[0:16, tc * 512:(tc + 1) * 512],
                                                   pc3[4 + tc][0:16, :], rbS[:])
                    inst.then_inc(s_dv, 1)
                    vv = DV3_LGEV

                    def step(inst):
                        nonlocal vv
                        vv += 1
                        inst.then_inc(s_dv, 1)
                        v.wait_ge(s_dv, vv)

                    v.wait_ge(s_pe, PE3_FWDT)
                    step(v.tensor_copy(fl(ptm), pc3[0][:, 0:256]))
                    step(v.reduce_max(aux[:, :, 0:1], ptm[:], axis=AX.X))
                    step(v.tensor_tensor(psv[:], ptm[:],
                                         aux[:, :, 0:1].broadcast_to([128, 16, 16]), OP.subtract))
                    assert vv == DV3_PRESOFT
                    v.wait_ge(s_sc, SC3_SEXP)
                    step(v.tensor_copy(flsh[:], aux[:, :, 0]))
                    step(v.reduce_sum(aux[:, :, 1:2], ptm2[:], axis=AX.X))
                    step(v.reciprocal(fl(aux[:, :, 2:3]), fl(aux[:, :, 1:2])))
                    step(v.tensor_tensor(ptm[:], ptm2[:], aux[:, :, 2:3].broadcast_to([128, 16, 16]), OP.mult))
                    step(v.reduce_max(aux[:, :, 0:1], ptm[:], axis=AX.X))
                    step(v.tensor_tensor(ctm[:], ptm[:], aux[:, :, 0:1].broadcast_to([128, 16, 16]), OP.is_equal))
                    step(v.scalar_tensor_tensor(psv[:], ctm[:], -1e30, ptm[:], OP.mult, OP.add))
                    step(v.reduce_max(aux[:, :, 1:2], psv[:], axis=AX.X))
                    step(v.tensor_tensor(ctm[:], ptm[:], aux[:, :, 1:2].broadcast_to([128, 16, 16]), OP.is_ge))
                    step(v.tensor_tensor(aux[:, :, 3:4], aux[:, :, 0:1], aux[:, :, 1:2], OP.add))
                    step(v.tensor_scalar_add(aux[:, :, 0:1], aux[:, :, 3:4], 1e-6))
                    step(v.reciprocal(fl(aux[:, :, 3:4]), fl(aux[:, :, 0:1])))
                    step(v.tensor_tensor(psv[:], ctm[:], ptm[:], OP.mult))
                    step(v.tensor_tensor(ctm[:], psv[:], aux[:, :, 3:4].broadcast_to([128, 16, 16]), OP.mult))
                    step(v.tensor_copy(flsh[:], aux[:, :, 0]))
                    assert vv == DV3_SMAX
                    v.wait_ge(s_pe, PE3_BACKT)
                    for i in range(4):
                        inst = v.tensor_copy(cfm[0:16, i * 512:(i + 1) * 512], pc3[4 + i][0:16, :])
                    inst.then_inc(s_dv, 1)

                @blk3.scalar
                def _(s):
                    s.wait_ge(s_dv, DV3_PRESOFT)
                    s.copy(flsh[:, 0:1], fl(psv)[:, 0:1])
                    s.activation(fl(ptm2), fl(psv), AF.Exp)
                    s.copy(flsh[:, 0:1], fl(ptm2)[:, 255:256]).then_inc(s_sc, 1)

                @blk3.sync
                def _(sy):
                    sy.wait_ge(s_dv, DV3_CFMEV)
                    sy.dma_start(comb_d[:], cfm[:]).then_inc(s_dc, 16)
                    sy.wait_ge(s_dc, 16)
                    if "ptm" in tap_d:
                        sy.dma_start(tap_d["ptm"][:], fl(ptm)).then_inc(s_do, 16)
                    if "psv" in tap_d:
                        sy.dma_start(tap_d["psv"][:], fl(psv)).then_inc(s_do, 16)
                    if "ptm2" in tap_d:
                        sy.dma_start(tap_d["ptm2"][:], fl(ptm2)).then_inc(s_do, 16)
                    if "ctm" in tap_d:
                        sy.dma_start(tap_d["ctm"][:], fl(ctm)).then_inc(s_do, 16)
                    if "lg" in tap_d:
                        sy.dma_start(tap_d["lg"][:], lg[:]).then_inc(s_do, 16)
                    if "comb" in tap_d:
                        sy.dma_start(tap_d["comb"][:], cfm[:]).then_inc(s_do, 16)
                    if "ln2x" in tap_d:
                        sy.dma_start(tap_d["ln2x"].rearrange("(cc p) t -> p cc t", p=128),
                                     ln2x[:]).then_inc(s_do, 16)

            if stop_after == "router":
                with nc.Block() as blk9:
                    @blk9.sync
                    def _(sy):
                        sy.dma_start(outv, acc[:]).then_inc(s_do, 16)
                        sy.wait_ge(s_do, 16 * (1 + ("lg" in tap_d) + ("comb" in tap_d) + ("ln2x" in tap_d) + n_chunk_taps))
                return nc

            with ExitStack() as sb4:
                eb4 = sb4.enter_context
                h = eb4(nc.sbuf_tensor("h", [128, ICC, T], BF16))
                cb = [eb4(nc.sbuf_tensor(f"cb{i}", [128, 4, 512], BF16)) for i in range(2)]
                sg = [eb4(nc.sbuf_tensor(f"sg{i}", [128, 512], F32)) for i in range(2)]
                dmy = eb4(nc.sbuf_tensor("dmy", [128, 512], F32))
                pc4 = [eb4(nc.psum_tensor(f"pc4{i}", [128, 512], F32)) for i in range(8)]
                blk4 = eb4(nc.Block())

                def wait_slot(t, m):
                    t.wait_ge(s_dsl[m % 4], 16 * (m // 4 + 1))

                def cb_count(e):
                    if e == 15:
                        return 1
                    return (e // 2 + 1) if e % 2 == 0 else ((e + 1) // 2 + 1)

                DW_BASE = 15
                gp_seq = ["comb15", "comb0", "comb1"]
                for m in range(4, 3 * NE):
                    e_of = m // 3
                    if m % 3 == 0 and e_of >= 2:
                        gp_seq.append(f"comb{e_of}")
                    gp_seq.append(f"m{m}")
                gp_idx = {nm: DW_BASE + i for i, nm in enumerate(gp_seq)}
                def dwt(nm): return 16 * (gp_idx[nm] + 1)

                def consumed_ms(m):
                    e, j = m // 3, m % 3
                    if j == 0: return pe_gate(e, 3)
                    if j == 1: return pe_up(e, 3)
                    return pe_down(e, 7)

                @blk4.gpsimd
                def _(g):
                    for nm in gp_seq:
                        if nm.startswith("comb"):
                            e = 15 if nm == "comb15" else int(nm[4:])
                            buf = cb[1] if nm == "comb15" else cb[e % 2]
                            if nm == "comb1":
                                g.wait_ge(s_dv, dv_id(3))
                            elif nm != "comb15" and nm != "comb0" and e >= 2:
                                g.wait_ge(s_dv, dv_h(e - 2, 3, 3))
                            g.dma_start(fl(buf[:]),
                                        comb_d[e][None, :].broadcast_to([128, T])
                                        ).then_inc(s_cb[1 if nm == "comb15" else e % 2], 16)
                        else:
                            m = int(nm[1:])
                            e, j = m // 3, m % 3
                            g.wait_ge(s_pe, consumed_ms(m - 4))
                            src = (gw_d, uw_d, dw_d)[j]
                            view = src[e].rearrange("(cc p) o -> p cc o", p=128)
                            g.dma_start(slot_ap(m), view).then_inc(s_dsl[m % 4], 16)

                @blk4.tensor
                def _(t):
                    t.wait_ge(s_dwf, 16 * 5)
                    for tc in range(4):
                        t0 = 2 + tc * 512
                        if tc >= 1:
                            t.wait_ge(s_dv, dv_dash(tc - 1, 7))
                        for ocI in range(ICC):
                            for kk in range(3):
                                for kc in range(CC):
                                    inst = t.matmul(pc4[ocI][:],
                                                    cwS[:, kk * 8 + kc, ocI * 128:(ocI + 1) * 128],
                                                    ln2x[:, kc, t0 - 2 + kk: t0 - 2 + kk + 512],
                                                    start=(kk == 0 and kc == 0),
                                                    stop=(kk == 2 and kc == CC - 1))
                        inst.then_inc(s_pe, 1)
                        for ocI in range(ICC):
                            for kc in range(CC):
                                inst = t.matmul(pc4[4 + ocI][:],
                                                suS[:, kc, ocI * 128:(ocI + 1) * 128],
                                                ln2x[:, kc, t0:t0 + 512],
                                                start=(kc == 0), stop=(kc == CC - 1))
                        inst.then_inc(s_pe, 1)
                        for oc in range(CC):
                            t.wait_ge(s_sc, sc_sish(tc, 3))
                            t.wait_ge(s_dv, dv_hsh(tc, 3))
                            if oc >= 4:
                                t.wait_ge(s_dv, dv_dash(tc, oc - 4))
                            for kcI in range(ICC):
                                inst = t.matmul(pc4[oc % 4][:],
                                                sdS[:, kcI, oc * 128:(oc + 1) * 128],
                                                h[:, kcI, tc * 512:(tc + 1) * 512],
                                                start=(kcI == 0), stop=(kcI == ICC - 1))
                            inst.then_inc(s_pe, 1)
                    for e in range(NE):
                        for ocI in range(ICC):
                            wait_slot(t, 3 * e)
                            if e == 0 and ocI == 0:
                                t.wait_ge(s_dv, dv_dash(3, 7))
                            elif ocI == 0:
                                t.wait_ge(s_dv, dv_dadd(e - 1, 6))
                            else:
                                t.wait_ge(s_sc, sc_silu(e, ocI - 1, 3))
                            for kc in range(CC):
                                for tc in range(4):
                                    inst = t.matmul(pc4[tc][:],
                                                    slot_ap(3 * e)[:, kc, ocI * 128:(ocI + 1) * 128],
                                                    ln2x[:, kc, 2 + tc * 512: 2 + (tc + 1) * 512],
                                                    start=(kc == 0), stop=(kc == CC - 1))
                            inst.then_inc(s_pe, 1)
                            wait_slot(t, 3 * e + 1)
                            if e == 0 and ocI == 0:
                                pass
                            elif ocI == 0:
                                t.wait_ge(s_dv, dv_dadd(e - 1, 7))
                            else:
                                t.wait_ge(s_dv, dv_h(e, ocI - 1, 3))
                            for kc in range(CC):
                                for tc in range(4):
                                    inst = t.matmul(pc4[4 + tc][:],
                                                    slot_ap(3 * e + 1)[:, kc, ocI * 128:(ocI + 1) * 128],
                                                    ln2x[:, kc, 2 + tc * 512: 2 + (tc + 1) * 512],
                                                    start=(kc == 0), stop=(kc == CC - 1))
                            inst.then_inc(s_pe, 1)
                        wait_slot(t, 3 * e + 2)
                        t.wait_ge(s_dv, dv_h(e, 3, 3))
                        for oc in range(CC):
                            if oc >= 2:
                                t.wait_ge(s_dv, dv_dadd(e, oc - 2))
                            for kcI in range(ICC):
                                for tc in range(4):
                                    inst = t.matmul(pc4[(oc % 2) * 4 + tc][:],
                                                    slot_ap(3 * e + 2)[:, kcI, oc * 128:(oc + 1) * 128],
                                                    h[:, kcI, tc * 512:(tc + 1) * 512],
                                                    start=(kcI == 0), stop=(kcI == ICC - 1))
                            inst.then_inc(s_pe, 1)

                @blk4.vector
                def _(v):
                    v.wait_ge(s_cb[1], 16)
                    for tc in range(4):
                        cbc = cb[1][:, tc, :].unsqueeze(1).broadcast_to([128, CC, 512])
                        hf = fl(h[:]).bitcast(F32).rearrange("p (cc t) -> p cc t", t=512)
                        v.tensor_tensor(hf[:], ln2x[:, :, 2 + tc * 512: 2 + (tc + 1) * 512], cbc, OP.mult)
                        id_out = acc[:, :, tc * 512:(tc + 1) * 512] if "id" in parts else hf[:]
                        v.tensor_tensor(id_out, acc[:, :, tc * 512:(tc + 1) * 512],
                                        hf[:], OP.add).then_inc(s_dv, 1)
                    for tc in range(4):
                        for ocI in range(ICC):
                            idx = 4 * tc + ocI
                            sl = sg[idx % 2]
                            v.wait_ge(s_sc, sc_sish(tc, ocI))
                            if ocI == 0:
                                v.wait_ge(s_pe, pe_upsh(tc))
                            v.tensor_tensor(sl[:], sl[:], pc4[4 + ocI][:], OP.mult)
                            v.tensor_copy(h[:, ocI, tc * 512:(tc + 1) * 512], sl[:]).then_inc(s_dv, 1)
                        for oc in range(CC):
                            v.wait_ge(s_pe, pe_dnsh(tc, oc))
                            ac = acc[:, oc, tc * 512:(tc + 1) * 512]
                            dst = ac if "sh" in parts else dmy[:]
                            v.tensor_tensor(dst, ac, pc4[oc % 4][:], OP.add).then_inc(s_dv, 1)
                    for e in range(NE):
                        v.wait_ge(s_cb[e % 2], 16 * cb_count(e))
                        for ocI in range(ICC):
                            v.wait_ge(s_pe, pe_up(e, ocI))
                            for tc in range(4):
                                idx = 16 * e + 4 * ocI + tc
                                sl = sg[idx % 2]
                                v.wait_ge(s_sc, sc_silu(e, ocI, tc))
                                v.tensor_tensor(sl[:], sl[:], pc4[4 + tc][:], OP.mult)
                                v.tensor_tensor(sl[:], sl[:], cb[e % 2][:, tc, :], OP.mult)
                                v.tensor_copy(h[:, ocI, tc * 512:(tc + 1) * 512], sl[:]).then_inc(s_dv, 1)
                        for oc in range(CC):
                            v.wait_ge(s_pe, pe_down(e, oc))
                            for tc in range(4):
                                ac = acc[:, oc, tc * 512:(tc + 1) * 512]
                                dst = ac if "ex" in parts else dmy[:]
                                inst = v.tensor_tensor(dst, ac, pc4[(oc % 2) * 4 + tc][:], OP.add)
                            inst.then_inc(s_dv, 1)

                @blk4.scalar
                def _(s):
                    for tc in range(4):
                        for ocI in range(ICC):
                            idx = 4 * tc + ocI
                            if idx >= 2:
                                s.wait_ge(s_dv, sg_h_milestone(idx - 2))
                            s.wait_ge(s_pe, pe_conv(tc))
                            s.activation(sg[idx % 2][:], pc4[ocI][:], AF.Silu).then_inc(s_sc, 1)
                    for e in range(NE):
                        for ocI in range(ICC):
                            s.wait_ge(s_pe, pe_gate(e, ocI))
                            for tc in range(4):
                                idx = 16 + 16 * e + 4 * ocI + tc
                                s.wait_ge(s_dv, sg_h_milestone(idx - 2))
                                s.activation(sg[idx % 2][:], pc4[tc][:], AF.Silu).then_inc(s_sc, 1)

                @blk4.sync
                def _(sy):
                    n_do = 1 + ("lg" in tap_d) + ("comb" in tap_d) + ("ln2x" in tap_d) + ("cb15" in tap_d) + ("ptm" in tap_d) + ("ctm" in tap_d) + ("psv" in tap_d) + ("ptm2" in tap_d) + n_chunk_taps
                    if "cb15" in tap_d:
                        sy.wait_ge(s_dv, dv_id(3))
                        sy.dma_start(tap_d["cb15"][:], fl(cb[1][:])).then_inc(s_do, 16)
                    sy.wait_ge(s_dv, dv_dadd(NE - 1, 7))
                    sy.dma_start(outv, acc[:]).then_inc(s_do, 16)
                    sy.wait_ge(s_do, 16 * n_do)

    return nc


def _pack_par(tf, td, tmk, tmv, tmr):
    cols = []
    for a in (tf, td, tmk, tmv, tmr):
        cols.append(np.ascontiguousarray(np.asarray(a, np.float32).reshape(CC, 128).T))
    return np.ascontiguousarray(np.concatenate(cols, axis=1))


def make_in_maps(inputs):
    shared = {
        "wk": np.ascontiguousarray(np.asarray(inputs["L0_attn_W_key"], np.float32).T),
        "wv": np.ascontiguousarray(np.asarray(inputs["L0_attn_W_value"], np.float32).T),
        "wr": np.ascontiguousarray(np.asarray(inputs["L0_attn_W_receptance"], np.float32).T),
        "wo": np.ascontiguousarray(np.asarray(inputs["L0_attn_W_output"], np.float32).T),
        "par": _pack_par(inputs["L0_attn_time_first"], inputs["L0_attn_time_decay"],
                         np.asarray(inputs["L0_attn_time_mix_k"]).ravel(),
                         np.asarray(inputs["L0_attn_time_mix_v"]).ravel(),
                         np.asarray(inputs["L0_attn_time_mix_r"]).ravel()),
        "idm": np.eye(128, dtype=np.float32),
        "gw": np.ascontiguousarray(np.asarray(inputs["L0_ffn_expert_gate_w"], np.float32)),
        "uw": np.ascontiguousarray(np.asarray(inputs["L0_ffn_expert_up_w"], np.float32)),
        "dw": np.ascontiguousarray(np.asarray(inputs["L0_ffn_expert_down_w"], np.float32)),
        "cw": np.ascontiguousarray(np.asarray(inputs["L0_ffn_conv_w"], np.float32).transpose(2, 1, 0)),
        "su": np.ascontiguousarray(np.asarray(inputs["L0_ffn_shared_up_w"], np.float32)),
        "sd": np.ascontiguousarray(np.asarray(inputs["L0_ffn_shared_down_w"], np.float32)),
        "rw": np.ascontiguousarray(np.asarray(inputs["L0_ffn_router_w"], np.float32)),
        "rb": np.ascontiguousarray(np.asarray(inputs["L0_ffn_router_bias"], np.float32).reshape(16, 1)),
    }
    x = np.asarray(inputs["L0_attn_x"], np.float32)
    return [dict(shared, x=np.ascontiguousarray(x[b].T)) for b in range(B)]


def kernel(**inputs):
    nc = build()
    in_maps = make_in_maps(inputs)
    res = run_bass_kernel_spmd(nc, in_maps, core_ids=list(range(B)))
    return np.ascontiguousarray(
        np.stack([res.results[b]["out"].T for b in range(B)], axis=0)).astype(np.float32)
